# version 04
# baseline (speedup 1.0000x reference)
import contextlib
import math

import numpy as np

import concourse.bass as bass
import concourse.mybir as mybir
from concourse.ap import AP as APc
from concourse.bass_utils import run_bass_kernel_spmd

F32 = mybir.dt.float32
BF16 = mybir.dt.bfloat16
I32 = mybir.dt.int32
AF = mybir.ActivationFunctionType
OP = mybir.AluOpType

NCORE = 8
D = 4096
R = 1024
C = 32
SEQ = 4096
NST = 64
LN_EPS = 1e-5
EXP_SHIFT = 12.0
L0_FFN_ONLY = 0


class Sem:
    def __init__(self, es, nc, name):
        self.h = es.enter_context(nc.semaphore(name))
        self.n = 0

    def inc(self, inst, k=1):
        inst.then_inc(self.h, k)
        self.n += k
        return inst



def _l0_consts():
    f = np.arange(SEQ, dtype=np.float64)
    om = np.exp(-2j * np.pi * f / SEQ)
    g1 = (1.0 - om) / (1.0 + om)
    cc = 2.0 / (1.0 + om)
    cre = cc.real.astype(np.float32).reshape(C, 128).T.copy()
    cim = cc.imag.astype(np.float32).reshape(C, 128).T.copy()
    n = np.arange(NST, dtype=np.float64)
    w64 = np.exp(2j * np.pi * np.outer(n, n) / NST)
    tw = np.exp(2j * np.pi * np.outer(n, n) / SEQ)
    w2 = np.exp(2j * np.pi * np.outer(n, n) / NST) / SEQ
    return {
        "g1c": np.stack([g1.real, g1.imag]).astype(np.float32),
        "ckre": cre, "ckim": cim,
        "w64cat": np.concatenate([w64.real, w64.imag], 1).astype(np.float32),
        "twre": tw.real.astype(np.float32), "twim": tw.imag.astype(np.float32),
        "w2cat": np.concatenate([w2.real, w2.imag], 1).astype(np.float32),
    }



def _emit_ln_finalize(nc, block, st_ps, ps_rep, m_bc, r_bc, mrow, vrow, srow,
                      epsc, ones1f, s_pe, s_ve, s_se):
    pe_in = s_pe.n

    def _(v):
        def VS(inst):
            s_ve.inc(inst)
            v.wait_ge(s_ve.h, s_ve.n)
        v.wait_ge(s_pe.h, pe_in)
        for h in range(2):
            v.tensor_scalar_mul(mrow[:, h * 512:(h + 1) * 512], st_ps[h][:, :],
                                -1.0 / D)
            VS(v.tensor_scalar_mul(vrow[:, h * 512:(h + 1) * 512],
                                   st_ps[2 + h][:, :], 1.0 / D))
        VS(v.tensor_tensor(srow[:, :], mrow[:, :], mrow[:, :], OP.mult))
        s_ve.inc(v.tensor_tensor(vrow[:, :], vrow[:, :], srow[:, :], OP.subtract))
    block.vector(_)
    ve1 = s_ve.n

    def _(s):
        s.wait_ge(s_ve.h, ve1)
        s_se.inc(s.activation(vrow[:, :], vrow[:, :], AF.Sqrt, bias=epsc[0:1, 0:1]))
    block.scalar(_)
    se1 = s_se.n

    def _(v):
        v.wait_ge(s_se.h, se1)
        s_ve.inc(v.reciprocal(vrow[:, :], vrow[:, :]))
    block.vector(_)
    ve2 = s_ve.n
    for h in range(2):
        for which, row, dst in ((0, mrow, m_bc), (1, vrow, r_bc)):
            def _(t, h=h, which=which, row=row, _ven=s_ve.n):
                t.wait_ge(s_ve.h, _ven)
                t.wait_ge(s_ve.h, ve2)
                s_pe.inc(t.matmul(ps_rep[which][:, 0:512], ones1f[:, :],
                                  row[0:1, h * 512:(h + 1) * 512],
                                  start=True, stop=True))
            block.tensor(_)
            pe_h = s_pe.n

            def _(v, h=h, which=which, dst=dst, pe_h=pe_h):
                v.wait_ge(s_pe.h, pe_h)
                s_ve.inc(v.tensor_copy(dst[:, h * 512:(h + 1) * 512],
                                       ps_rep[which][:, 0:512]))
            block.vector(_)



def build_l0(upto=99, debug=False):
    nc = bass.Bass("TRN2", target_bir_lowering=False, debug=False)
    ap = {}

    def din(name, shape, dt=F32):
        ap[name] = nc.dram_tensor(name, shape, dt, kind="ExternalInput").ap()

    def dout(name, shape, dt):
        ap[name] = nc.dram_tensor(name, shape, dt, kind="ExternalOutput").ap()

    din("xt", [D, R])
    for n, s in [("lam_re", [NST]), ("lam_im", [NST]), ("p_re", [NST]),
                 ("p_im", [NST]), ("q_re", [NST]), ("q_im", [NST]),
                 ("ct_re", [NST]), ("ct_im", [NST]), ("bp", [NST]),
                 ("d_attn", [1]), ("log_step", [1]),
                 ("vct_re", [NST, NST]), ("vct_im", [NST, NST]),
                 ("g1c", [2, SEQ]), ("ckre", [128, C]), ("ckim", [128, C]),
                 ("w64cat", [NST, 128]), ("twre", [NST, NST]),
                 ("twim", [NST, NST]), ("w2cat", [NST, 128]),
                 ("vht", [D, D]), ("ut", [D, D]),
                 ("s_top", [NST]), ("alpha", [NST]), ("beta", [NST]),
                 ("s_tail", [D - NST]),
                 ("vhr4t", [D, 4]), ("r4t", [4, 4]), ("utr4t", [4, D]),
                 ("d4", [4]), ("str4", [4]), ("bias0", [D])]:
        din(n, s)
    dout("x1t", [D, R], BF16)
    dout("xn1t", [D, R], BF16)
    if debug:
        dout("dbg_karr", [128, 8 * C], F32)
        dout("dbg_att", [128, 2 * C], F32)
        dout("dbg_am", [NST, 2 * NST], F32)
        dout("dbg_kt", [NST, NST], F32)
        dout("dbg_v8", [NST, 8], F32)
        dout("dbg_bv", [NST, 2], F32)
        dout("dbg_ct", [NST, 2], F32)
        dout("dbg_gb", [NST, 2 * 1024], F32)
        dout("dbg_den", [NST, 2 * 1024], F32)
        dout("dbg_scl", [1, 1], F32)
        dout("dbg_invd", [NST, 2 * 1024], F32)
        dout("dbg_xn0t", [D, R], BF16)
        dout("dbg_t", [D, R], BF16)
        dout("dbg_kbig", [128, SEQ], BF16)
    ap["dr_a"] = nc.dram_tensor("dr_a", [2 * SEQ], F32, kind="Internal").ap()
    ap["dr_kpad"] = nc.dram_tensor("dr_kpad", [127 + SEQ], F32, kind="Internal").ap()

    with contextlib.ExitStack() as es:
        emit_l0(nc, es, ap, upto, debug)
    return nc


def emit_l0(nc, es, ap, upto=99, debug=False):
    def sb(name, shape, dt, stack=None):
        return (stack or es).enter_context(nc.sbuf_tensor(name, shape, dt))

    FB = 1024
    NFB = SEQ // FB

    xn0t = sb("xn0t", [128, C, R], BF16)
    kbig = sb("kbig", [128, SEQ], BF16)
    xchk = [sb("xchk0", [128, R], F32)]
    xchk = xchk * 2
    xbfc = [sb(f"xbfc{i}", [128, R], BF16) for i in range(2)]
    xsq = [sb("xsq0", [128, R], BF16)]
    xsq = xsq * 2
    evb = [sb(f"evb{i}", [128, R], F32) for i in range(2)]
    dscr = [sb(f"dscr{i}", [128, R], F32) for i in range(2)]
    m_bc = sb("m_bc", [128, R], F32)
    r_bc = sb("r_bc", [128, R], F32)
    mrow = sb("mrow", [1, R], F32)
    vrow = sb("vrow", [1, R], F32)
    srow = sb("srow", [1, R], F32)
    onesb = sb("onesb", [128, 1], BF16)
    ones1f = sb("ones1f", [1, 128], F32)
    epsc = sb("epsc", [1, 1], F32)
    s_sb = sb("s_sb", [128, C], F32)
    bias0_sb = sb("bias0_sb", [128, C], F32)
    datt = sb("datt", [128, 1], F32)
    vhr4 = sb("vhr4", [128, C * 4], BF16)
    r4 = sb("r4", [4, 4], BF16)
    sd4 = sb("sd4", [4, 1], F32)

    with nc.Block() as block:
        s_gld = Sem(es, nc, "gld")
        s_sld = Sem(es, nc, "sld")
        s_pe = Sem(es, nc, "pe")
        s_ve = Sem(es, nc, "ve")
        s_se = Sem(es, nc, "se")
        s_kst = Sem(es, nc, "kst")
        s_st = Sem(es, nc, "st")

        T, V, SC, G, SY = (block.tensor, block.vector, block.scalar,
                           block.gpsimd, block.sync)

        def _mkdrain():
            def _(sy):
                for s in (s_gld, s_sld, s_kst, s_pe, s_ve, s_se, s_st):
                    if s.n:
                        sy.wait_ge(s.h, s.n)
            SY(_)

        def _(g):
            s_gld.inc(g.dma_start(out=vhr4[:, :].rearrange("p (a n) -> p a n", n=4),
                                  in_=ap["vhr4t"].rearrange("(a p) n -> p a n", p=128)),
                      16)
            s_gld.inc(g.dma_start(out=r4[:, :], in_=ap["r4t"]), 16)
        G(_)

        with contextlib.ExitStack() as esk:
            kg = {}
            for n, shp in [("lam", [NST, 2]), ("pq", [NST, 4]), ("ct", [NST, 2]),
                           ("bp", [NST, 1]), ("vct", [NST, 2 * NST]),
                           ("bv", [NST, 2]), ("v8", [NST, 8]), ("vs", [NST, 1]),
                           ("ls", [1, 1]), ("scl", [1, 1]), ("dat0", [1, 1]),
                           ("g1r", [1, SEQ]), ("g1i", [1, SEQ]), ("gb", [NST, 2 * FB]),
                           ("den", [NST, 2 * FB]), ("mag", [NST, FB]),
                           ("invd", [NST, 2 * FB]),
                           ("am", [NST, 2 * NST]), ("w64", [NST, 128]),
                           ("bm", [NST, 2 * NST]), ("tw", [NST, 2 * NST]),
                           ("cm", [NST, 2 * NST]), ("w2", [NST, 128]),
                           ("b1", [NST, 128]), ("b2", [NST, 128]),
                           ("k2d", [NST, NST]), ("s4s", [NST, 5]),
                           ("karr", [128, 8 * C]), ("att", [128, 2 * C]),
                           ("ck", [128, 2 * C]), ("sc8", [128, 8 * C]),
                           ("kt", [NST, NST]), ("z127", [1, 127]), ("cpk", [128, 8]),
                           ("idf", [128, 128])]:
                kg[n] = esk.enter_context(nc.sbuf_tensor("kg_" + n, shp, F32))
            karr, att, ck, sc8, kt, z127, identf = (
                kg["karr"], kg["att"], kg["ck"], kg["sc8"], kg["kt"],
                kg["z127"], kg["idf"])

            def _(sy):
                pairs = [
                    (kg["lam"][:, 0:1], ap["lam_re"]), (kg["lam"][:, 1:2], ap["lam_im"]),
                    (kg["pq"][:, 0:1], ap["p_re"]), (kg["pq"][:, 1:2], ap["p_im"]),
                    (kg["pq"][:, 2:3], ap["q_re"]), (kg["pq"][:, 3:4], ap["q_im"]),
                    (kg["ct"][:, 0:1], ap["ct_re"]), (kg["ct"][:, 1:2], ap["ct_im"]),
                    (kg["bp"][:, :], ap["bp"]),
                    (kg["vct"][:, 0:NST], ap["vct_re"]),
                    (kg["vct"][:, NST:], ap["vct_im"]),
                    (kg["ls"][:, :], ap["log_step"]),
                    (kg["g1r"][:, :], ap["g1c"][0:1, :]),
                    (kg["g1i"][:, :], ap["g1c"][1:2, :]),
                    (ck[:, 0:C], ap["ckre"]), (ck[:, C:], ap["ckim"]),
                    (kg["w64"][:, :], ap["w64cat"]),
                    (kg["tw"][:, 0:NST], ap["twre"]),
                    (kg["tw"][:, NST:], ap["twim"]),
                    (kg["w2"][:, :], ap["w2cat"]),
                    (s_sb[64:128, 0:1], ap["s_tail"][0:NST]),
                    (s_sb[:, 1:C], APc(ap["s_tail"].tensor, NST, [[1, 128], [128, C - 1]])),
                    (kg["s4s"][:, 0:1], ap["s_top"]),
                    (kg["s4s"][:, 1:2], ap["alpha"]),
                    (kg["s4s"][:, 2:3], ap["beta"]),
                    (kg["s4s"][0:4, 3:4], ap["d4"]),
                    (kg["s4s"][0:4, 4:5], ap["str4"]),
                    (bias0_sb[:, :], APc(ap["bias0"].tensor, 0, [[1, 128], [128, C]])),
                    (kg["dat0"][:, :], ap["d_attn"]),
                ]
                for o, i in pairs:
                    s_sld.inc(sy.dma_start(out=o, in_=i,
                                           allow_slow_non_contiguous=True), 16)
            SY(_)
            sld0 = s_sld.n

            def _(v):
                def VS(inst):
                    s_ve.inc(inst)
                    v.wait_ge(s_ve.h, s_ve.n)
                v.memset(onesb[:, :], 1.0)
                v.memset(ones1f[:, :], 1.0)
                v.memset(epsc[:, :], LN_EPS)
                v.memset(z127[:, :], 0.0)
                v.wait_ge(s_sld.h, sld0)
                VS(v.tensor_tensor(s_sb[0:NST, 0:1], kg["s4s"][:, 0:1],
                                   kg["s4s"][:, 1:2], OP.mult))
                VS(v.tensor_tensor(s_sb[0:NST, 0:1], s_sb[0:NST, 0:1],
                                   kg["s4s"][:, 2:3], OP.add))
                VS(v.tensor_relu(s_sb[0:NST, 0:1], s_sb[0:NST, 0:1]))
                VS(v.tensor_relu(sd4[:, :], kg["s4s"][0:4, 3:4]))
                s_ve.inc(v.tensor_tensor(sd4[:, :], sd4[:, :],
                                         kg["s4s"][0:4, 4:5], OP.mult))
            V(_)

            def _(g):
                g.memset(identf[:, :], 1.0)
                s_gld.inc(g.affine_select(identf[:, :], identf[:, :], [[-1, 128]],
                                          OP.is_equal, 0.0, base=0,
                                          channel_multiplier=1), 1)
            G(_)
            g_id = s_gld.n
            if upto < 0.1:
                _mkdrain()
                return

            with contextlib.ExitStack() as esp:
                pk1 = esp.enter_context(nc.psum_tensor("pk1", [128, 8], F32))
                pk2 = esp.enter_context(nc.psum_tensor("pk2", [128, 8], F32))
                pkA = esp.enter_context(nc.psum_tensor("pkA", [NST, 128], F32))
                pkB = esp.enter_context(nc.psum_tensor("pkB", [NST, 128], F32))
                pkR = esp.enter_context(nc.psum_tensor("pkR", [NST, 512], F32))

                def _(t):
                    t.wait_ge(s_sld.h, sld0)
                    t.matmul(pk1[:, 0:1], ones1f[:, :], kg["dat0"][:, :],
                             start=True, stop=True)
                    t.matmul(pk2[0:NST, 0:1], kg["vct"][:, 0:NST], kg["bp"][:, :],
                             start=True, stop=True)
                    s_pe.inc(t.matmul(pk2[0:NST, 1:2], kg["vct"][:, NST:],
                                      kg["bp"][:, :], start=True, stop=True))
                T(_)
                pe_b = s_pe.n

                def _(v):
                    def VS(inst):
                        s_ve.inc(inst)
                        v.wait_ge(s_ve.h, s_ve.n)
                    v.wait_ge(s_pe.h, pe_b)
                    v.tensor_copy(datt[:, :], pk1[:, 0:1])
                    VS(v.tensor_copy(kg["bv"][:, :], pk2[0:NST, 0:2]))
                    cr, ci = kg["ct"][:, 0:1], kg["ct"][:, 1:2]
                    pr, pi = kg["pq"][:, 0:1], kg["pq"][:, 1:2]
                    qr, qi = kg["pq"][:, 2:3], kg["pq"][:, 3:4]
                    br, bi = kg["bv"][:, 0:1], kg["bv"][:, 1:2]
                    v8, vs = kg["v8"], kg["vs"]
                    for col, (ar, ai, xr, xi) in enumerate(
                            [(cr, ci, br, bi), (cr, ci, pr, pi),
                             (qr, qi, br, bi), (qr, qi, pr, pi)]):
                        v.tensor_tensor(v8[:, col:col + 1], ar, xr, OP.mult)
                        VS(v.tensor_tensor(vs[:, :], ai, xi, OP.mult))
                        VS(v.tensor_tensor(v8[:, col:col + 1], v8[:, col:col + 1],
                                           vs[:, :], OP.add))
                        v.tensor_tensor(v8[:, 4 + col:5 + col], ar, xi, OP.mult)
                        VS(v.tensor_tensor(vs[:, :], ai, xr, OP.mult))
                        VS(v.tensor_tensor(v8[:, 4 + col:5 + col],
                                           v8[:, 4 + col:5 + col], vs[:, :],
                                           OP.subtract))
                    s_ve.inc(v.tensor_copy(kg["vs"][:, :], kg["vs"][:, :]))
                V(_)
                ve_v8 = s_ve.n
                if upto < 0.2:
                    _mkdrain()
                    return

                def _(s):
                    s.wait_ge(s_sld.h, sld0)
                    s_se.inc(s.activation(kg["scl"][:, :], kg["ls"][:, :], AF.Exp,
                                          scale=-1.0))
                SC(_)
                se_g = s_se.n

                def _(v):
                    def VS(inst):
                        s_ve.inc(inst)
                        v.wait_ge(s_ve.h, s_ve.n)
                    v.wait_ge(s_se.h, se_g)
                    VS(v.tensor_scalar_mul(kg["scl"][:, :], kg["scl"][:, :], 2.0))
                    v.tensor_scalar(kg["g1r"][:, :], kg["g1r"][:, :],
                                    kg["scl"][0:1, 0:1], None, OP.mult)
                    s_ve.inc(v.tensor_scalar(kg["g1i"][:, :], kg["g1i"][:, :],
                                             kg["scl"][0:1, 0:1], None, OP.mult))
                V(_)
                ve_g1 = s_ve.n
                if upto < 0.25:
                    _mkdrain()
                    return

                for fb in range(NFB):
                    if upto < 0.26 + 0.03 * fb:
                        _mkdrain()
                        return
                    for w in range(2):
                        for j in range(FB // 512):
                            def _(t, fb=fb, w=w, j=j):
                                t.wait_ge(s_ve.h, s_ve.n)
                                t.wait_ge(s_ve.h, ve_g1)
                                if fb > 0:
                                    t.wait_ge(s_ve.h, ve_den[fb - 1])
                                grow = kg["g1r"] if w == 0 else kg["g1i"]
                                s_pe.inc(t.matmul(
                                    pkR[:, :], ones1f[:, 0:NST],
                                    grow[0:1, fb * FB + j * 512:
                                         fb * FB + (j + 1) * 512],
                                    start=True, stop=True))
                            T(_)
                            pe_g = s_pe.n

                            def _(v, w=w, j=j, pe_g=pe_g):
                                v.wait_ge(s_pe.h, pe_g)
                                s_ve.inc(v.tensor_copy(
                                    kg["gb"][:, w * FB + j * 512:w * FB + (j + 1) * 512],
                                    pkR[:, :]))
                            V(_)

                    ve_gb = s_ve.n
                    if upto < 0.27 + 0.03 * fb:
                        _mkdrain()
                        return
                    if fb == 0:
                        ve_den = {}
                        pe_cau = {}

                    def _(v, fb=fb):
                        def VS(inst):
                            s_ve.inc(inst)
                            v.wait_ge(s_ve.h, s_ve.n)
                        v.wait_ge(s_ve.h, ve_gb)
                        if fb > 0:
                            v.wait_ge(s_pe.h, pe_cau[fb - 1])
                        den, gb, mag, invd = (kg["den"], kg["gb"], kg["mag"],
                                              kg["invd"])
                        v.tensor_scalar(den[:, 0:FB], gb[:, 0:FB],
                                        kg["lam"][:, 0:1], None, OP.subtract)
                        VS(v.tensor_scalar(den[:, FB:], gb[:, FB:],
                                           kg["lam"][:, 1:2], None, OP.subtract))
                        v.tensor_tensor(mag[:, :], den[:, 0:FB], den[:, 0:FB],
                                        OP.mult)
                        VS(v.tensor_tensor(invd[:, 0:FB], den[:, FB:], den[:, FB:],
                                           OP.mult))
                        VS(v.tensor_tensor(mag[:, :], mag[:, :], invd[:, 0:FB],
                                           OP.add))
                        VS(v.reciprocal(mag[:, :], mag[:, :]))
                        v.tensor_tensor(invd[:, 0:FB], den[:, 0:FB], mag[:, :],
                                        OP.mult)
                        s_ve.inc(v.scalar_tensor_tensor(invd[:, FB:], den[:, FB:],
                                                        -1.0, mag[:, :], OP.mult,
                                                        OP.mult))
                    V(_)
                    ve_iv = s_ve.n
                    if upto < 0.28 + 0.03 * fb:
                        _mkdrain()
                        return

                    for fcl in range(FB // 128):
                        fc = fb * (FB // 128) + fcl

                        def _(t, fcl=fcl, fc=fc, ve_iv=ve_iv):
                            t.wait_ge(s_ve.h, s_ve.n)
                            t.wait_ge(s_ve.h, max(ve_iv, ve_v8))
                            t.matmul(pk1[:, :],
                                     kg["invd"][:, fcl * 128:(fcl + 1) * 128],
                                     kg["v8"][:, :], start=True, stop=True)
                            s_pe.inc(t.matmul(
                                pk2[:, :],
                                kg["invd"][:, FB + fcl * 128:FB + (fcl + 1) * 128],
                                kg["v8"][:, :], start=True, stop=True))
                        T(_)
                        pe_c = s_pe.n

                        def _(v, fc=fc, pe_c=pe_c):
                            v.wait_ge(s_pe.h, pe_c)
                            s_ve.inc(v.tensor_copy(kg["cpk"][:, :], pk1[:, :]))
                            v.wait_ge(s_ve.h, s_ve.n)
                            v.tensor_tensor(karr[:, fc * 4:fc * 4 + 4],
                                            kg["cpk"][:, 0:4], pk2[:, 4:8],
                                            OP.subtract)
                            s_ve.inc(v.tensor_tensor(
                                karr[:, 4 * C + fc * 4:4 * C + fc * 4 + 4],
                                kg["cpk"][:, 4:8], pk2[:, 0:4], OP.add))
                        V(_)
                    ve_den[fb] = s_ve.n
                    pe_cau[fb] = s_pe.n
                ve_ka = s_ve.n
                if upto < 0.4:
                    _mkdrain()
                    return

                def _(v):
                    def VS(inst):
                        s_ve.inc(inst)
                        v.wait_ge(s_ve.h, s_ve.n)
                    v.wait_ge(s_ve.h, ve_ka)
                    ka = karr[:, 0:4 * C].rearrange("p (a w) -> p a w", w=4)
                    kb = karr[:, 4 * C:].rearrange("p (a w) -> p a w", w=4)
                    S = [sc8[:, i * C:(i + 1) * C] for i in range(8)]
                    u, vv, r, wre, wim, tre, tim, tmp = S
                    are, aim = att[:, 0:C], att[:, C:]
                    cre, cim = ck[:, 0:C], ck[:, C:]
                    v.tensor_scalar_add(u, ka[:, :, 3], 1.0)
                    VS(v.tensor_copy(vv, kb[:, :, 3]))
                    v.tensor_tensor(r, u, u, OP.mult)
                    VS(v.tensor_tensor(tmp, vv, vv, OP.mult))
                    VS(v.tensor_tensor(r, r, tmp, OP.add))
                    VS(v.reciprocal(r, r))
                    v.tensor_tensor(wre, u, r, OP.mult)
                    VS(v.scalar_tensor_tensor(wim, vv, -1.0, r, OP.mult, OP.mult))
                    v.tensor_tensor(tre, ka[:, :, 1], ka[:, :, 2], OP.mult)
                    VS(v.tensor_tensor(tmp, kb[:, :, 1], kb[:, :, 2], OP.mult))
                    VS(v.tensor_tensor(tre, tre, tmp, OP.subtract))
                    v.tensor_tensor(tim, ka[:, :, 1], kb[:, :, 2], OP.mult)
                    VS(v.tensor_tensor(tmp, kb[:, :, 1], ka[:, :, 2], OP.mult))
                    VS(v.tensor_tensor(tim, tim, tmp, OP.add))
                    v.tensor_tensor(u, tre, wre, OP.mult)
                    VS(v.tensor_tensor(tmp, tim, wim, OP.mult))
                    VS(v.tensor_tensor(u, u, tmp, OP.subtract))
                    v.tensor_tensor(vv, tre, wim, OP.mult)
                    VS(v.tensor_tensor(tmp, tim, wre, OP.mult))
                    VS(v.tensor_tensor(vv, vv, tmp, OP.add))
                    v.tensor_tensor(tre, ka[:, :, 0], u, OP.subtract)
                    VS(v.tensor_tensor(tim, kb[:, :, 0], vv, OP.subtract))
                    v.tensor_tensor(are, cre, tre, OP.mult)
                    VS(v.tensor_tensor(tmp, cim, tim, OP.mult))
                    VS(v.tensor_tensor(are, are, tmp, OP.subtract))
                    v.tensor_tensor(aim, cre, tim, OP.mult)
                    VS(v.tensor_tensor(tmp, cim, tre, OP.mult))
                    s_ve.inc(v.tensor_tensor(aim, aim, tmp, OP.add))
                V(_)
                ve_at = s_ve.n

                def _(t):
                    t.wait_ge(s_ve.h, ve_at)
                    t.wait_ge(s_gld.h, g_id)
                    s_pe.inc(t.matmul(pkA[:, :], att[:, :], identf[:, :],
                                      start=True, stop=True))
                T(_)
                pe_at = s_pe.n

                def _(v):
                    v.wait_ge(s_pe.h, pe_at)
                    s_ve.inc(v.tensor_copy(kg["am"][:, :], pkA[:, :]))
                V(_)
                ve_att = s_ve.n

                def _(sy):
                    sy.wait_ge(s_ve.h, ve_att)
                    s_kst.inc(sy.dma_start(
                        out=APc(ap["dr_a"].tensor, 0, [[128, C], [1, 128]]),
                        in_=kg["am"][0:C, :]), 16)
                    s_kst.inc(sy.dma_start(
                        out=APc(ap["dr_a"].tensor, SEQ, [[128, C], [1, 128]]),
                        in_=kg["am"][C:2 * C, :]), 16)
                SY(_)
                kst_a = s_kst.n

                def _(sy):
                    sy.wait_ge(s_kst.h, kst_a)
                    s_sld.inc(sy.dma_start(
                        out=kg["am"][:, 0:NST],
                        in_=APc(ap["dr_a"].tensor, 0, [[NST, NST], [1, NST]])), 16)
                    s_sld.inc(sy.dma_start(
                        out=kg["am"][:, NST:],
                        in_=APc(ap["dr_a"].tensor, SEQ, [[NST, NST], [1, NST]])), 16)
                SY(_)
                sld_am = s_sld.n
                if upto < 0.6:
                    _mkdrain()
                    return

                def _(t):
                    t.wait_ge(s_sld.h, sld_am)
                    t.matmul(pkA[:, :], kg["am"][:, 0:NST], kg["w64"][:, :],
                             start=True, stop=True)
                    s_pe.inc(t.matmul(pkB[:, :], kg["am"][:, NST:], kg["w64"][:, :],
                                      start=True, stop=True))
                T(_)
                pe_s1 = s_pe.n

                def _(v):
                    def VS(inst):
                        s_ve.inc(inst)
                        v.wait_ge(s_ve.h, s_ve.n)
                    v.wait_ge(s_pe.h, pe_s1)
                    v.tensor_copy(kg["b1"][:, :], pkA[:, :])
                    VS(v.tensor_copy(kg["b2"][:, :], pkB[:, :]))
                    bm = kg["bm"]
                    v.tensor_tensor(bm[:, 0:NST], kg["b1"][:, 0:NST],
                                    kg["b2"][:, NST:], OP.subtract)
                    VS(v.tensor_tensor(bm[:, NST:], kg["b1"][:, NST:],
                                       kg["b2"][:, 0:NST], OP.add))
                    cm, tw = kg["cm"], kg["tw"]
                    v.tensor_tensor(cm[:, 0:NST], bm[:, 0:NST], tw[:, 0:NST],
                                    OP.mult)
                    VS(v.tensor_tensor(kg["b1"][:, 0:NST], bm[:, NST:],
                                       tw[:, NST:], OP.mult))
                    VS(v.tensor_tensor(cm[:, 0:NST], cm[:, 0:NST],
                                       kg["b1"][:, 0:NST], OP.subtract))
                    v.tensor_tensor(cm[:, NST:], bm[:, 0:NST], tw[:, NST:], OP.mult)
                    VS(v.tensor_tensor(kg["b1"][:, 0:NST], bm[:, NST:],
                                       tw[:, 0:NST], OP.mult))
                    s_ve.inc(v.tensor_tensor(cm[:, NST:], cm[:, NST:],
                                             kg["b1"][:, 0:NST], OP.add))
                V(_)
                ve_cm = s_ve.n

                def _(t):
                    t.wait_ge(s_ve.h, ve_cm)
                    t.matmul(pkA[:, :], kg["cm"][:, 0:NST], kg["w2"][:, :],
                             start=True, stop=True)
                    s_pe.inc(t.matmul(pkB[:, :], kg["cm"][:, NST:], kg["w2"][:, :],
                                      start=True, stop=True))
                T(_)
                pe_s2 = s_pe.n

                def _(v):
                    v.wait_ge(s_pe.h, pe_s2)
                    v.tensor_copy(kg["b1"][:, :], pkA[:, :])
                    s_ve.inc(v.tensor_copy(kg["b2"][:, :], pkB[:, :]))
                    v.wait_ge(s_ve.h, s_ve.n)
                    s_ve.inc(v.tensor_tensor(kg["k2d"][:, :], kg["b1"][:, 0:NST],
                                             kg["b2"][:, NST:], OP.subtract))
                V(_)
                ve_k2 = s_ve.n

                def _(t):
                    t.wait_ge(s_ve.h, ve_k2)
                    s_pe.inc(t.matmul(pkA[0:NST, 0:NST], kg["k2d"][:, :],
                                      identf[0:NST, 0:NST], start=True, stop=True))
                T(_)
                pe_kt = s_pe.n

                def _(v):
                    v.wait_ge(s_pe.h, pe_kt)
                    s_ve.inc(v.tensor_copy(kt[:, :], pkA[0:NST, 0:NST]))
                V(_)
                ve_kt = s_ve.n

                def _(sy):
                    sy.wait_ge(s_ve.h, ve_kt)
                    s_kst.inc(sy.dma_start(out=ap["dr_kpad"][0:127],
                                           in_=z127[0:1, :]), 16)
                    s_kst.inc(sy.dma_start(
                        out=APc(ap["dr_kpad"].tensor, 127, [[NST, NST], [1, NST]]),
                        in_=kt[:, :]), 16)
                SY(_)
                kst_k = s_kst.n
                if upto < 0.8:
                    _mkdrain()
                    return

                if debug:
                    def _(sy):
                        sy.wait_ge(s_ve.h, s_ve.n)
                        s_st.inc(sy.dma_start(out=ap["dbg_karr"], in_=karr[:, :]), 16)
                        s_st.inc(sy.dma_start(out=ap["dbg_att"], in_=att[:, :]), 16)
                        s_st.inc(sy.dma_start(out=ap["dbg_am"], in_=kg["am"][:, :]), 16)
                        s_st.inc(sy.dma_start(out=ap["dbg_kt"], in_=kt[:, :]), 16)
                        s_st.inc(sy.dma_start(out=ap["dbg_v8"], in_=kg["v8"][:, :]), 16)
                        s_st.inc(sy.dma_start(out=ap["dbg_invd"], in_=kg["invd"][:, :]), 16)
                        s_st.inc(sy.dma_start(out=ap["dbg_bv"], in_=kg["bv"][:, :]), 16)
                        s_st.inc(sy.dma_start(out=ap["dbg_ct"], in_=kg["ct"][:, :]), 16)
                        s_st.inc(sy.dma_start(out=ap["dbg_gb"], in_=kg["gb"][:, :]), 16)
                        s_st.inc(sy.dma_start(out=ap["dbg_den"], in_=kg["den"][:, :]), 16)
                        s_st.inc(sy.dma_start(out=ap["dbg_scl"], in_=kg["scl"][:, :]), 16)
                    SY(_)

                def _(g):
                    g.wait_ge(s_kst.h, kst_k)
                    for p in range(128):
                        s_gld.inc(g.dma_start(
                            out=kbig[p:p + 1, :],
                            in_=ap["dr_kpad"][127 - p:127 - p + SEQ]), 16)
                G(_)

            if upto < 1:
                def _(sy):
                    sy.wait_ge(s_gld.h, s_gld.n)
                SY(_)
                return
            with contextlib.ExitStack() as esa:
                st0 = [esa.enter_context(nc.psum_tensor(f"st0_{i}", [1, 512], F32))
                       for i in range(4)]
                psr = [esa.enter_context(nc.psum_tensor(f"psr{i}", [128, 512], F32))
                       for i in range(2)]
                pe_use = {}
                for dc in range(C):
                    def _(g, dc=dc):
                        if dc >= 2:
                            g.wait_ge(s_pe.h, pe_use[dc - 2])
                        s_gld.inc(g.dma_start(out=xbfc[dc % 2][:, :],
                                              in_=ap["xt"][dc * 128:(dc + 1) * 128, :]),
                                  16)
                    G(_)
                    gld_c = s_gld.n

                    def _(s, dc=dc, gld_c=gld_c):
                        s.wait_ge(s_gld.h, gld_c)
                        if dc >= 1:
                            s.wait_ge(s_pe.h, pe_use[dc - 1])
                        s_se.inc(s.activation(xsq[dc % 2][:, :], xbfc[dc % 2][:, :],
                                              AF.Square))
                    SC(_)
                    se_c = s_se.n

                    def _(t, dc=dc, gld_c=gld_c, se_c=se_c, _v0=s_ve.n):
                        t.wait_ge(s_gld.h, gld_c)
                        t.wait_ge(s_se.h, se_c)
                        if dc == 0:
                            t.wait_ge(s_ve.h, _v0)
                        for h in range(2):
                            t.matmul(st0[h][:, :], onesb[:, :],
                                     xbfc[dc % 2][:, h * 512:(h + 1) * 512],
                                     start=(dc == 0), stop=(dc == C - 1))
                        inst = None
                        for h in range(2):
                            inst = t.matmul(st0[2 + h][:, :], onesb[:, :],
                                            xsq[dc % 2][:, h * 512:(h + 1) * 512],
                                            start=(dc == 0), stop=(dc == C - 1))
                        s_pe.inc(inst)
                    T(_)
                    pe_use[dc] = s_pe.n

                _emit_ln_finalize(nc, block, st0, psr, m_bc, r_bc, mrow, vrow,
                                  srow, epsc, ones1f, s_pe, s_ve, s_se)

                if upto < 2:
                    def _(sy):
                        sy.wait_ge(s_ve.h, s_ve.n)
                    SY(_)
                    return
                ve_use = {}
                for dc in range(C):
                    def _(sy, dc=dc):
                        if dc >= 1:
                            sy.wait_ge(s_ve.h, ve_use[dc - 1])
                        s_sld.inc(sy.dma_start(out=xchk[dc % 2][:, :],
                                               in_=ap["xt"][dc * 128:(dc + 1) * 128, :]),
                                  16)
                    SY(_)
                    sld_c = s_sld.n

                    def _(v, dc=dc, sld_c=sld_c):
                        v.wait_ge(s_sld.h, sld_c)
                        s_ve.inc(v.tensor_tensor(dscr[dc % 2][:, :],
                                                 xchk[dc % 2][:, :],
                                                 m_bc[:, :], OP.add))
                        v.wait_ge(s_ve.h, s_ve.n)
                        s_ve.inc(v.tensor_tensor(xn0t[:, dc, :], dscr[dc % 2][:, :],
                                                 r_bc[:, :], OP.mult))
                    V(_)
                    ve_use[dc] = s_ve.n

        if upto < 3:
            def _(sy):
                sy.wait_ge(s_ve.h, s_ve.n)
            SY(_)
            return
        with contextlib.ExitStack() as esx:
            t_sb = sb("t_sb", [128, C, R], BF16, esx)
            wstr = [sb(f"wstr{i}", [128, C, 128], BF16, esx) for i in range(2)]
            utr4s = [sb(f"utr4s{i}", [4, 128], BF16, esx) for i in range(2)]
            t4 = sb("t4", [4, R], BF16, esx)
            m4s = sb("m4s", [4, R], BF16, esx)

            with contextlib.ExitStack() as esp1:
                ps_a = [esp1.enter_context(nc.psum_tensor(f"ps_a{i}", [128, R], F32))
                        for i in range(2)]
                pt4 = esp1.enter_context(nc.psum_tensor("pt4", [4, R], F32))

                ve_evac = {}
                pe_grp = {}
                ve_all_b = s_ve.n
                for jc in range(C):
                    def _(g, jc=jc):
                        if jc >= 2:
                            g.wait_ge(s_pe.h, pe_grp[jc - 2])
                        s_gld.inc(g.dma_start(
                            out=wstr[jc % 2][:, :, :],
                            in_=ap["vht"][:, jc * 128:(jc + 1) * 128]
                            .rearrange("(a p) n -> p a n", p=128)), 16)
                    G(_)
                    gld_w = s_gld.n

                    def _(t, jc=jc, gld_w=gld_w):
                        t.wait_ge(s_gld.h, gld_w)
                        if jc == 0:
                            t.wait_ge(s_ve.h, ve_all_b)
                        if jc >= 2:
                            t.wait_ge(s_ve.h, ve_evac[jc - 2])
                        inst = None
                        for ic in range(C):
                            for h in range(2):
                                inst = t.matmul(ps_a[jc % 2][:, h * 512:(h + 1) * 512],
                                                wstr[jc % 2][:, ic, :],
                                                xn0t[:, ic, h * 512:(h + 1) * 512],
                                                start=(ic == 0), stop=(ic == C - 1))
                            if jc == 0:
                                for h in range(2):
                                    inst = t.matmul(pt4[:, h * 512:(h + 1) * 512],
                                                    vhr4[:, ic * 4:(ic + 1) * 4],
                                                    xn0t[:, ic, h * 512:(h + 1) * 512],
                                                    start=(ic == 0), stop=(ic == C - 1))
                        s_pe.inc(inst)
                    T(_)
                    pe_grp[jc] = s_pe.n

                    def _(v, jc=jc):
                        v.wait_ge(s_pe.h, pe_grp[jc])
                        s_ve.inc(v.tensor_scalar(t_sb[:, jc, :], ps_a[jc % 2][:, :],
                                                 s_sb[:, jc:jc + 1], None, OP.mult))
                        if jc == 0:
                            s_ve.inc(v.tensor_copy(t4[:, :], pt4[:, :]))
                    V(_)
                    ve_evac[jc] = s_ve.n

                ve_t4 = s_ve.n

                def _(t):
                    t.wait_ge(s_ve.h, ve_t4)
                    inst = None
                    for h in range(2):
                        inst = t.matmul(pt4[:, h * 512:(h + 1) * 512], r4[:, :],
                                        t4[:, h * 512:(h + 1) * 512],
                                        start=True, stop=True)
                    s_pe.inc(inst)
                T(_)
                pe_m4 = s_pe.n

                def _(v):
                    v.wait_ge(s_pe.h, pe_m4)
                    s_ve.inc(v.tensor_scalar(m4s[:, :], pt4[:, :], sd4[:, 0:1],
                                             None, OP.mult))
                V(_)

            if upto < 4:
                def _(sy):
                    sy.wait_ge(s_ve.h, s_ve.n)
                SY(_)
                return
            with contextlib.ExitStack() as esp2:
                ps_a = [esp2.enter_context(nc.psum_tensor(f"ps_b{i}", [128, R], F32))
                        for i in range(2)]
                st1 = [esp2.enter_context(nc.psum_tensor(f"st1_{i}", [1, 512], F32))
                       for i in range(4)]
                ve_m4 = s_ve.n
                se_ev = {}
                ve_ev = {}
                pe_oc = {}
                pe_stat = {}
                st_x1 = {}
                for oc in range(C):
                    def _(g, oc=oc):
                        if oc < 2:
                            g.wait_ge(s_pe.h, pe_grp[C - 2 + oc])
                        if oc >= 2:
                            g.wait_ge(s_pe.h, pe_oc[oc - 2])
                        s_gld.inc(g.dma_start(
                            out=wstr[oc % 2][:, :, :],
                            in_=ap["ut"][:, oc * 128:(oc + 1) * 128]
                            .rearrange("(a p) n -> p a n", p=128)), 16)
                        s_gld.inc(g.dma_start(
                            out=utr4s[oc % 2][:, :],
                            in_=ap["utr4t"][:, oc * 128:(oc + 1) * 128]), 16)
                    G(_)
                    gld_u = s_gld.n

                    def _(sy, oc=oc):
                        if oc == 0:
                            sy.wait_ge(s_ve.h, ve_m4)
                        if oc >= 1:
                            sy.wait_ge(s_ve.h, ve_ev[oc - 1])
                        s_sld.inc(sy.dma_start(out=xchk[oc % 2][:, :],
                                               in_=ap["xt"][oc * 128:(oc + 1) * 128, :]),
                                  16)
                    SY(_)
                    sld_x = s_sld.n

                    def _(t, oc=oc, gld_u=gld_u):
                        t.wait_ge(s_gld.h, gld_u)
                        if oc == 0:
                            t.wait_ge(s_ve.h, ve_m4)
                        if oc >= 2:
                            t.wait_ge(s_se.h, se_ev[oc - 2])
                            t.wait_ge(s_ve.h, ve_ev[oc - 2])
                        p = ps_a[oc % 2]
                        for ic in range(C):
                            for h in range(2):
                                t.matmul(p[:, h * 512:(h + 1) * 512],
                                         wstr[oc % 2][:, ic, :],
                                         t_sb[:, ic, h * 512:(h + 1) * 512],
                                         start=(ic == 0), stop=False)
                        for h in range(2):
                            t.matmul(p[:, h * 512:(h + 1) * 512],
                                     utr4s[oc % 2][:, :],
                                     m4s[:, h * 512:(h + 1) * 512],
                                     start=False, stop=False)
                        inst = None
                        for jc in range(oc + 1):
                            d = oc - jc
                            last = (jc == oc)
                            for h in range(2):
                                inst = t.matmul(p[:, h * 512:(h + 1) * 512],
                                                kbig[:, d * 128:(d + 1) * 128],
                                                xn0t[:, jc, h * 512:(h + 1) * 512],
                                                start=False,
                                                stop=last)
                        s_pe.inc(inst)
                    T(_)
                    pe_oc[oc] = s_pe.n

                    def _(s, oc=oc):
                        s_se.inc(s.engine_nop() if False else s.nop())
                    SC(_)
                    se_ev[oc] = s_se.n

                    def _(v, oc=oc):
                        if oc >= 2:
                            v.wait_ge(s_ve.h, ve_ev[oc - 2])
                        s_ve.inc(v.tensor_scalar(dscr[oc % 2][:, :],
                                                 xn0t[:, oc, :], datt[:, 0:1],
                                                 None, OP.mult))
                    V(_)

                    def _(v, oc=oc):
                        v.wait_ge(s_pe.h, pe_oc[oc])
                        if oc >= 2:
                            v.wait_ge(s_st.h, st_x1[oc - 2])
                        s_ve.inc(v.tensor_scalar(evb[oc % 2][:, :],
                                                 ps_a[oc % 2][:, :],
                                                 bias0_sb[:, oc:oc + 1], None,
                                                 OP.add))
                    V(_)
                    ve_bias = s_ve.n

                    def _(v, oc=oc, sld_x=sld_x):
                        v.wait_ge(s_se.h, se_ev[oc])
                        v.wait_ge(s_sld.h, sld_x)
                        v.wait_ge(s_ve.h, ve_bias)
                        if oc >= 2:
                            v.wait_ge(s_pe.h, pe_stat[oc - 2])
                        if L0_FFN_ONLY in (0, 3):
                            s_ve.inc(v.tensor_tensor(evb[oc % 2][:, :],
                                                     evb[oc % 2][:, :],
                                                     dscr[oc % 2][:, :], OP.add))
                            v.wait_ge(s_ve.h, s_ve.n)
                        if L0_FFN_ONLY in (0, 2):
                            s_ve.inc(v.tensor_tensor(evb[oc % 2][:, :],
                                                     evb[oc % 2][:, :],
                                                     xchk[oc % 2][:, :], OP.add))
                            v.wait_ge(s_ve.h, s_ve.n)
                        s_ve.inc(v.tensor_copy(xbfc[oc % 2][:, :], evb[oc % 2][:, :]))
                    V(_)
                    ve_ev[oc] = s_ve.n

                    def _(s, oc=oc):
                        s.wait_ge(s_ve.h, ve_ev[oc])
                        if oc >= 1:
                            s.wait_ge(s_pe.h, pe_stat[oc - 1])
                        s_se.inc(s.activation(xsq[oc % 2][:, :], evb[oc % 2][:, :],
                                              AF.Square))
                    SC(_)
                    se_sq = s_se.n

                    def _(t, oc=oc, se_sq=se_sq):
                        t.wait_ge(s_se.h, se_sq)
                        for h in range(2):
                            t.matmul(st1[h][:, :], onesb[:, :],
                                     xbfc[oc % 2][:, h * 512:(h + 1) * 512],
                                     start=(oc == 0), stop=(oc == C - 1))
                        inst = None
                        for h in range(2):
                            inst = t.matmul(st1[2 + h][:, :], onesb[:, :],
                                            xsq[oc % 2][:, h * 512:(h + 1) * 512],
                                            start=(oc == 0), stop=(oc == C - 1))
                        s_pe.inc(inst)
                    T(_)
                    pe_stat[oc] = s_pe.n

                    def _(g, oc=oc):
                        g.wait_ge(s_ve.h, ve_ev[oc])
                        s_st.inc(g.dma_start(out=ap["x1t"][oc * 128:(oc + 1) * 128, :],
                                             in_=evb[oc % 2][:, :]), 16)
                    G(_)
                    st_x1[oc] = s_st.n

                _emit_ln_finalize(nc, block, st1, ps_a, m_bc, r_bc, mrow, vrow,
                                  srow, epsc, ones1f, s_pe, s_ve, s_se)

                if upto < 5:
                    def _(sy):
                        sy.wait_ge(s_st.h, s_st.n)
                    SY(_)
                    return
                st_all = s_st.n
                ve_c = {}
                st_c = {}
                for oc in range(C):
                    def _(g, oc=oc):
                        if oc == 0:
                            g.wait_ge(s_st.h, st_all)
                        if oc >= 1:
                            g.wait_ge(s_ve.h, ve_c[oc - 1])
                        s_gld.inc(g.dma_start(out=xchk[oc % 2][:, :],
                                              in_=ap["x1t"][oc * 128:(oc + 1) * 128, :]),
                                  16)
                    G(_)
                    gld_x1 = s_gld.n

                    def _(v, oc=oc, gld_x1=gld_x1):
                        v.wait_ge(s_gld.h, gld_x1)
                        if oc >= 2:
                            v.wait_ge(s_st.h, st_c[oc - 2])
                        s_ve.inc(v.tensor_tensor(dscr[oc % 2][:, :],
                                                 xchk[oc % 2][:, :],
                                                 m_bc[:, :], OP.add))
                        v.wait_ge(s_ve.h, s_ve.n)
                        s_ve.inc(v.tensor_tensor(xbfc[oc % 2][:, :],
                                                 dscr[oc % 2][:, :],
                                                 r_bc[:, :], OP.mult))
                    V(_)
                    ve_c[oc] = s_ve.n

                    def _(sy, oc=oc):
                        sy.wait_ge(s_ve.h, ve_c[oc])
                        s_st.inc(sy.dma_start(
                            out=ap["xn1t"][oc * 128:(oc + 1) * 128, :],
                            in_=xbfc[oc % 2][:, :]), 16)
                    SY(_)
                    st_c[oc] = s_st.n

                if debug:
                    def _(sy):
                        sy.wait_ge(s_ve.h, s_ve.n)
                        sy.wait_ge(s_pe.h, s_pe.n)
                        for dc in range(C):
                            s_st.inc(sy.dma_start(
                                out=ap["dbg_xn0t"][dc * 128:(dc + 1) * 128, :],
                                in_=xn0t[:, dc, :]), 16)
                            s_st.inc(sy.dma_start(
                                out=ap["dbg_t"][dc * 128:(dc + 1) * 128, :],
                                in_=t_sb[:, dc, :]), 16)
                        s_st.inc(sy.dma_start(out=ap["dbg_kbig"][:, :],
                                              in_=kbig[:, :]), 16)
                    SY(_)

                st_fin = s_st.n

                def _(sy):
                    sy.wait_ge(s_st.h, st_fin)
                SY(_)


RT = 8192
HD = 128


def build_l2():
    nc = bass.Bass("TRN2", target_bir_lowering=False, debug=False)
    ap = {}

    def din(name, shape, dt=F32):
        ap[name] = nc.dram_tensor(name, shape, dt, kind="ExternalInput").ap()

    din("xn1t", [D, RT], BF16)
    din("wq", [D, 512])
    din("wk", [D, HD])
    din("wv", [D, HD])
    din("cosT", [HD, SEQ])
    din("sinT", [HD, SEQ])
    din("ropeP", [HD, HD])
    ap["attn"] = nc.dram_tensor("attn", [512, RT], BF16, kind="ExternalOutput").ap()
    ap["dr_q"] = nc.dram_tensor("dr_q", [4, HD, RT], BF16, kind="Internal").ap()

    with contextlib.ExitStack() as es:
        emit_l2(nc, es, ap)
    return nc


def emit_l2(nc, es, ap):
    def sb(name, shape, dt, stack=None):
        return (stack or es).enter_context(nc.sbuf_tensor(name, shape, dt))

    wq_sb = sb("wq_sb", [128, C, 512], BF16)
    wk_sb = sb("wk_sb", [128, C, HD], BF16)
    wv_sb = sb("wv_sb", [128, C, HD], BF16)
    k_rot = sb("k_rot", [128, RT], BF16)
    v_sb = sb("v_sb", [128, RT], BF16)
    v_t = sb("v_t", [128, 64 * HD], BF16)
    xin = [sb(f"xin{i}", [128, C, 512], BF16) for i in range(2)]
    csb = [sb(f"csb{i}", [128, 512], F32) for i in range(2)]
    ssb = [sb(f"ssb{i}", [128, 512], F32) for i in range(2)]
    rraw = [sb(f"rraw{i}", [128, 512], F32) for i in range(2)]
    rt1 = sb("rt1", [128, 512], F32)
    rt2 = sb("rt2", [128, 512], F32)
    pconst = sb("pconst", [HD, HD], F32)
    identb = sb("identb", [128, 128], BF16)
    onesb = sb("onesb", [128, 1], BF16)
    ones1f = sb("ones1f", [1, 128], F32)
    ebias = sb("ebias", [128, 1], F32)
    masks = sb("masks", [128, 4 * 512], BF16)
    expS = [sb(f"expS{i}", [128, 512], BF16) for i in range(2)]
    qch = [sb(f"qch{i}", [128, 512], BF16) for i in range(2)]
    drow = sb("drow", [1, 512], F32)
    drow2 = sb("drow2", [1, 512], F32)
    bcsb = sb("bcsb", [128, 512], F32)
    obf = [sb(f"obf{i}", [128, 512], BF16) for i in range(2)]

    with nc.Block() as block:
        s_gld = Sem(es, nc, "gld")
        s_sld = Sem(es, nc, "sld")
        s_pe = Sem(es, nc, "pe")
        s_ve = Sem(es, nc, "ve")
        s_se = Sem(es, nc, "se")
        s_qst = Sem(es, nc, "qst")
        s_st = Sem(es, nc, "st")

        T, V, SC, G, SY = (block.tensor, block.vector, block.scalar,
                           block.gpsimd, block.sync)

        def _(g):
            s_gld.inc(g.dma_start(out=wq_sb[:, :, :],
                                  in_=ap["wq"].rearrange("(a p) n -> p a n", p=128)), 16)
            s_gld.inc(g.dma_start(out=wk_sb[:, :, :],
                                  in_=ap["wk"].rearrange("(a p) n -> p a n", p=128)), 16)
            s_gld.inc(g.dma_start(out=wv_sb[:, :, :],
                                  in_=ap["wv"].rearrange("(a p) n -> p a n", p=128)), 16)
            g.memset(identb[:, :], 1.0)
            s_gld.inc(g.affine_select(identb[:, :], identb[:, :], [[-1, 128]],
                                      OP.is_equal, 0.0, base=0,
                                      channel_multiplier=1), 1)
            inst = None
            for t in range(4):
                g.memset(masks[:, t * 512:(t + 1) * 512], 1.0)
                inst = g.affine_select(masks[:, t * 512:(t + 1) * 512],
                                       masks[:, t * 512:(t + 1) * 512],
                                       [[1, 512]], OP.is_ge, 0.0,
                                       base=-128 * t, channel_multiplier=-1)
            s_gld.inc(inst, 1)
        G(_)
        gld_setup = s_gld.n

        def _(sy):
            s_sld.inc(sy.dma_start(out=pconst[:, :], in_=ap["ropeP"]), 16)
        SY(_)
        sld_p = s_sld.n

        def _(v):
            v.memset(onesb[:, :], 1.0)
            v.memset(ones1f[:, :], 1.0)
            s_ve.inc(v.memset(ebias[:, :], -EXP_SHIFT))
        V(_)
        ve_setup = s_ve.n

        with contextlib.ExitStack() as esq:
            psm = [esq.enter_context(nc.psum_tensor(f"psm{i}", [128, 512], F32))
                   for i in range(2)]
            psp = esq.enter_context(nc.psum_tensor("psp", [128, 512], F32))

            pe_mc = {}
            ve_done = {}
            pe_psp = {}
            qst_ring = {}
            ve_psp_read = 0
            mlist = [("q", 0), ("q", 1), ("q", 2), ("q", 3), ("k", 0), ("v", 0)]

            for hb in range(16):
                r0 = hb * 512
                scol = (hb % 8) * 512

                def _(sy, hb=hb, r0=r0, scol=scol):
                    if hb >= 2:
                        sy.wait_ge(s_pe.h, pe_mc[(hb - 2, 5)])
                    s_sld.inc(sy.dma_start(out=xin[hb % 2][:, :, :],
                                           in_=APc(ap["xn1t"].tensor, r0,
                                                   [[RT, 128], [RT * 128, C],
                                                    [1, 512]])), 16)
                    if hb >= 2:
                        sy.wait_ge(s_ve.h, ve_done[(hb - 2, 4)])
                    s_sld.inc(sy.dma_start(out=csb[hb % 2][:, :],
                                           in_=ap["cosT"][:, scol:scol + 512]), 16)
                    s_sld.inc(sy.dma_start(out=ssb[hb % 2][:, :],
                                           in_=ap["sinT"][:, scol:scol + 512]), 16)
                SY(_)
                sld_h = s_sld.n

                for mi, (kind, mc) in enumerate(mlist):
                    wsrc = {"q": wq_sb, "k": wk_sb, "v": wv_sb}[kind]

                    def _(t, hb=hb, mi=mi, mc=mc, wsrc=wsrc, sld_h=sld_h):
                        t.wait_ge(s_sld.h, sld_h)
                        if hb == 0 and mi == 0:
                            t.wait_ge(s_gld.h, gld_setup)
                        key = (hb, mi - 2) if mi >= 2 else (hb - 1, mi + 4)
                        if key in ve_done:
                            t.wait_ge(s_ve.h, ve_done[key])
                        inst = None
                        for kc in range(C):
                            inst = t.matmul(psm[mi % 2][:, :],
                                            wsrc[:, kc, mc * 128:(mc + 1) * 128],
                                            xin[hb % 2][:, kc, :],
                                            start=(kc == 0), stop=(kc == C - 1))
                        s_pe.inc(inst)
                    T(_)
                    pe_mc[(hb, mi)] = s_pe.n

                    if kind == "v":
                        def _(v, hb=hb, mi=mi, r0=r0):
                            v.wait_ge(s_pe.h, pe_mc[(hb, mi)])
                            s_ve.inc(v.tensor_copy(v_sb[:, r0:r0 + 512],
                                                   psm[mi % 2][:, :]))
                        V(_)
                        ve_done[(hb, mi)] = s_ve.n
                    else:
                        def _(v, hb=hb, mi=mi, sld_h=sld_h):
                            v.wait_ge(s_pe.h, pe_mc[(hb, mi)])
                            v.wait_ge(s_sld.h, sld_h)
                            s_ve.inc(v.tensor_copy(rraw[mi % 2][:, :],
                                                   psm[mi % 2][:, :]))
                        V(_)
                        ve_done[(hb, mi)] = s_ve.n
                        ve_raw = s_ve.n

                        def _(t, hb=hb, mi=mi, ve_raw=ve_raw,
                              _vr=ve_psp_read):
                            t.wait_ge(s_ve.h, max(ve_raw, _vr))
                            if hb == 0 and mi == 0:
                                t.wait_ge(s_sld.h, sld_p)
                            s_pe.inc(t.matmul(psp[:, :], pconst[:, :],
                                              rraw[mi % 2][:, :],
                                              start=True, stop=True))
                        T(_)
                        pe_psp[(hb, mi)] = s_pe.n

                        def _(v, hb=hb, mi=mi, kind=kind, r0=r0):
                            v.wait_ge(s_pe.h, pe_psp[(hb, mi)])
                            s_ve.inc(v.tensor_tensor(rt1[:, :], rraw[mi % 2][:, :],
                                                     csb[hb % 2][:, :], OP.mult))
                            s_ve.inc(v.tensor_tensor(rt2[:, :], psp[:, :],
                                                     ssb[hb % 2][:, :], OP.mult))
                            v.wait_ge(s_ve.h, s_ve.n)
                            if kind == "k":
                                out = k_rot[:, r0:r0 + 512]
                            else:
                                out = qch[mi % 2][:, :]
                                if qst_ring.get(mi % 2):
                                    v.wait_ge(s_qst.h, qst_ring[mi % 2])
                            s_ve.inc(v.tensor_tensor(out, rt1[:, :], rt2[:, :],
                                                     OP.add))
                        V(_)
                        ve_psp_read = s_ve.n
                        ve_done[(hb, mi)] = s_ve.n
                        ve_rot = s_ve.n

                        if kind == "q":
                            def _(g, mi=mi, mc=mc, r0=r0, ve_rot=ve_rot):
                                g.wait_ge(s_ve.h, ve_rot)
                                s_qst.inc(g.dma_start(
                                    out=ap["dr_q"][mc, :, r0:r0 + 512],
                                    in_=qch[mi % 2][:, :]), 16)
                            G(_)
                            qst_ring[mi % 2] = s_qst.n

        with contextlib.ExitStack() as esv:
            pst = esv.enter_context(nc.psum_tensor("pst", [128, 512], F32))
            vt_ve = {}
            last_ve = s_ve.n
            for grp in range(16):
                def _(t, grp=grp, last_ve=last_ve):
                    t.wait_ge(s_ve.h, last_ve if grp == 0 else vt_ve[grp - 1])
                    inst = None
                    for j in range(4):
                        skc = grp * 4 + j
                        inst = t.matmul(pst[:, j * 128:(j + 1) * 128],
                                        v_sb[:, skc * 128:(skc + 1) * 128],
                                        identb[:, :], start=True, stop=True)
                    s_pe.inc(inst)
                T(_)
                pe_vt = s_pe.n

                def _(v, grp=grp, pe_vt=pe_vt):
                    v.wait_ge(s_pe.h, pe_vt)
                    s_ve.inc(v.tensor_copy(v_t[:, grp * 512:(grp + 1) * 512],
                                           pst[:, :]))
                V(_)
                vt_ve[grp] = s_ve.n

        with contextlib.ExitStack() as esf:
            pss = [esf.enter_context(nc.psum_tensor(f"pss{i}", [128, 512], F32))
                   for i in range(2)]
            pso = esf.enter_context(nc.psum_tensor("pso", [128, 512], F32))
            psd = esf.enter_context(nc.psum_tensor("psd", [1, 512], F32))
            psb = esf.enter_context(nc.psum_tensor("psb", [128, 512], F32))

            qst_all = s_qst.n
            vt_all = s_ve.n
            blocks = [(sqc, b, h) for sqc in range(8) for b in range(2)
                      for h in range(4)]
            pe_S = {}
            se_exp = {}
            ve_mask = {}
            pe_AVk = {}
            pe_AV = {}
            ve_out = {}
            st_out = {}
            gld_q = {}
            exp_slot = {}
            av_slot = {}
            for bi, (sqc, b, h) in enumerate(blocks):
                nsk = 4 * sqc + 4
                scol = b * SEQ + sqc * 512

                def _(g, bi=bi, h=h, scol=scol):
                    if bi == 0:
                        g.wait_ge(s_qst.h, qst_all)
                    if bi >= 2:
                        g.wait_ge(s_pe.h, pe_AV[bi - 2])
                    s_gld.inc(g.dma_start(out=qch[bi % 2][:, :],
                                          in_=ap["dr_q"][h, :, scol:scol + 512]),
                              16)
                G(_)
                gld_q[bi] = s_gld.n

                for skc in range(nsk):
                    slot = skc % 2

                    def _(t, bi=bi, skc=skc, b=b, slot=slot,
                          _prev_exp=exp_slot.get(skc % 2)):
                        if skc == 0:
                            t.wait_ge(s_gld.h, gld_q[bi])
                            if bi == 0:
                                t.wait_ge(s_ve.h, vt_all)
                        if _prev_exp is not None:
                            t.wait_ge(s_se.h, _prev_exp)
                        s_pe.inc(t.matmul(pss[slot][:, :],
                                          k_rot[:, b * SEQ + skc * 128:
                                                b * SEQ + (skc + 1) * 128],
                                          qch[bi % 2][:, :],
                                          start=True, stop=True))
                    T(_)
                    pe_S[(bi, skc)] = s_pe.n

                    def _(s, bi=bi, skc=skc, slot=slot,
                          _prev_av=av_slot.get(skc % 2)):
                        s.wait_ge(s_pe.h, pe_S[(bi, skc)])
                        if _prev_av is not None:
                            s.wait_ge(s_pe.h, _prev_av)
                        s_se.inc(s.activation(expS[slot][:, :],
                                              pss[slot][:, :], AF.Exp,
                                              bias=ebias[:, 0:1],
                                              scale=float(1.0 / math.sqrt(HD))))
                    SC(_)
                    se_exp[(bi, skc)] = s_se.n
                    exp_slot[slot] = s_se.n

                    diag = skc - 4 * sqc
                    if diag >= 0:
                        def _(v, bi=bi, skc=skc, slot=slot, diag=diag):
                            v.wait_ge(s_se.h, se_exp[(bi, skc)])
                            s_ve.inc(v.tensor_tensor(
                                expS[slot][:, :], expS[slot][:, :],
                                masks[:, diag * 512:(diag + 1) * 512], OP.mult))
                        V(_)
                        ve_mask[(bi, skc)] = s_ve.n

                    def _(t, bi=bi, skc=skc, b=b, nsk=nsk, diag=diag, slot=slot):
                        if diag >= 0:
                            t.wait_ge(s_ve.h, ve_mask[(bi, skc)])
                        else:
                            t.wait_ge(s_se.h, se_exp[(bi, skc)])
                        if skc == 0 and bi >= 1:
                            t.wait_ge(s_ve.h, ve_out[bi - 1])
                        t.matmul(pso[:, :], v_t[:, (b * 32 + skc) * HD:
                                                 (b * 32 + skc + 1) * HD],
                                 expS[slot][:, :],
                                 start=(skc == 0), stop=(skc == nsk - 1))
                        s_pe.inc(t.matmul(psd[:, :], onesb[:, :],
                                          expS[slot][:, :],
                                          start=(skc == 0), stop=(skc == nsk - 1)))
                    T(_)
                    pe_AVk[(bi, skc)] = s_pe.n
                    av_slot[slot] = s_pe.n
                pe_AV[bi] = s_pe.n

                def _(v, bi=bi):
                    v.wait_ge(s_pe.h, pe_AV[bi])
                    s_ve.inc(v.tensor_copy(drow[:, :], psd[:, :]))
                    v.wait_ge(s_ve.h, s_ve.n)
                    s_ve.inc(v.reciprocal(drow2[:, :], drow[:, :]))
                V(_)
                ve_rec = s_ve.n

                def _(t, ve_rec=ve_rec):
                    t.wait_ge(s_ve.h, ve_rec)
                    s_pe.inc(t.matmul(psb[:, :], ones1f[:, :], drow2[0:1, :],
                                      start=True, stop=True))
                T(_)
                pe_bc = s_pe.n

                def _(v, bi=bi, pe_bc=pe_bc):
                    v.wait_ge(s_pe.h, pe_bc)
                    if bi >= 2:
                        v.wait_ge(s_st.h, st_out[bi - 2])
                    s_ve.inc(v.tensor_copy(bcsb[:, :], psb[:, :]))
                    v.wait_ge(s_ve.h, s_ve.n)
                    s_ve.inc(v.tensor_tensor(obf[bi % 2][:, :], pso[:, :],
                                             bcsb[:, :], OP.mult))
                V(_)
                ve_out[bi] = s_ve.n

                def _(sy, bi=bi, h=h, scol=scol):
                    sy.wait_ge(s_ve.h, ve_out[bi])
                    s_st.inc(sy.dma_start(
                        out=ap["attn"][h * 128:(h + 1) * 128, scol:scol + 512],
                        in_=obf[bi % 2][:, :]), 16)
                SY(_)
                st_out[bi] = s_st.n

            st_fin = s_st.n

            def _(sy):
                sy.wait_ge(s_st.h, st_fin)
            SY(_)


def rope_consts():
    inv_freq = 1.0 / (10000.0 ** (np.arange(0, HD, 2, dtype=np.float64) / HD))
    ang = np.arange(SEQ, dtype=np.float64)[None, :] * inv_freq[:, None]
    cosT = np.repeat(np.cos(ang), 2, axis=0).astype(np.float32)
    sinT = np.repeat(np.sin(ang), 2, axis=0).astype(np.float32)
    P = np.zeros((HD, HD), np.float32)
    for i in range(HD // 2):
        P[2 * i + 1, 2 * i] = -1.0
        P[2 * i, 2 * i + 1] = 1.0
    return cosT, sinT, P


def stage_l2_inputs(inp, xn1t_full, core):
    cosT, sinT, P = rope_consts()
    wqkv = np.asarray(inp["L1_attn_w_qkv"])
    q_sz = 4096
    return {
        "xn1t": xn1t_full,
        "wq": np.ascontiguousarray(wqkv[:, core * 512:(core + 1) * 512]),
        "wk": np.ascontiguousarray(wqkv[:, q_sz + core * HD:q_sz + (core + 1) * HD]),
        "wv": np.ascontiguousarray(wqkv[:, q_sz + 1024 + core * HD:
                                        q_sz + 1024 + (core + 1) * HD]),
        "cosT": cosT, "sinT": sinT, "ropeP": P,
    }


def build_l3():
    nc = bass.Bass("TRN2", target_bir_lowering=False, debug=False)
    ap = {}

    def din(name, shape, dt=F32):
        ap[name] = nc.dram_tensor(name, shape, dt, kind="ExternalInput").ap()

    din("xn1t", [D, RT], BF16)
    din("attnf", [D, RT], BF16)
    din("x1sl", [512, RT], BF16)
    din("wo", [D, 512])
    din("wqT", [D, 512], I32)
    din("sc2", [2, C, 512])
    din("zp2", [2, C, 512])
    din("sel2", [2, 128])
    din("bias1", [512])
    ap["outsl"] = nc.dram_tensor("outsl", [512, RT], F32, kind="ExternalOutput").ap()
    ap["dr_ffn"] = nc.dram_tensor("dr_ffn", [512, RT], BF16, kind="Internal").ap()

    with contextlib.ExitStack() as es:
        emit_l3(nc, es, ap)
    return nc


def emit_l3(nc, es, ap):
    def sb(name, shape, dt, stack=None):
        return (stack or es).enter_context(nc.sbuf_tensor(name, shape, dt))

    wqd = sb("wqd", [128, C, 512], BF16)
    wo_sb = sb("wo_sb", [128, C, 512], BF16)
    xin = [sb(f"xin{i}", [128, C, 512], BF16) for i in range(2)]
    sel2 = sb("sel2_sb", [2, 128], F32)
    scb = [sb(f"scb{i}", [2, 512], F32) for i in range(2)]
    zsb = [sb(f"zsb{i}", [2, 512], F32) for i in range(2)]
    bias1_sb = sb("bias1_sb", [128, 4], F32)
    c0 = sb("c0", [128, 512], F32)
    t1 = sb("t1", [128, 512], F32)
    evf = [sb(f"evf{i}", [128, 512], F32) for i in range(2)]
    ebf = [sb(f"ebf{i}", [128, 512], BF16) for i in range(2)]
    fch = [sb(f"fch{i}", [128, 512], F32) for i in range(2)]
    xch = [sb(f"xch{i}", [128, 512], F32) for i in range(2)]

    with nc.Block() as block:
        s_gld = Sem(es, nc, "gld")
        s_sld = Sem(es, nc, "sld")
        s_pe = Sem(es, nc, "pe")
        s_ve = Sem(es, nc, "ve")
        s_se = Sem(es, nc, "se")
        s_fst = Sem(es, nc, "fst")
        s_st = Sem(es, nc, "st")

        T, V, SC, G, SY = (block.tensor, block.vector, block.scalar,
                           block.gpsimd, block.sync)

        with contextlib.ExitStack() as esd:
            i32b = [sb(f"i32b{i}", [128, 512], I32, esd) for i in range(2)]
            psA = esd.enter_context(nc.psum_tensor("psA", [128, 512], F32))
            psB = esd.enter_context(nc.psum_tensor("psB", [128, 512], F32))

            def _(sy):
                s_sld.inc(sy.dma_start(out=sel2[:, :], in_=ap["sel2"]), 16)
                s_sld.inc(sy.dma_start(
                    out=bias1_sb[:, :],
                    in_=APc(ap["bias1"].tensor, 0, [[1, 128], [128, 4]]),
                    allow_slow_non_contiguous=True), 16)
            SY(_)
            sld0 = s_sld.n

            ve_dq = {}
            pe_dq = {}
            ve_zs = {}
            sld_sc = {}
            for kc in range(C):
                def _(sy, kc=kc):
                    if kc >= 2:
                        sy.wait_ge(s_pe.h, pe_dq[kc - 2])
                        sy.wait_ge(s_ve.h, ve_dq[kc - 2])
                    s_sld.inc(sy.dma_start(out=scb[kc % 2][:, :],
                                           in_=ap["sc2"][:, kc, :]), 16)
                    s_sld.inc(sy.dma_start(out=zsb[kc % 2][:, :],
                                           in_=ap["zp2"][:, kc, :]), 16)
                    s_sld.inc(sy.dma_start(out=i32b[kc % 2][:, :],
                                           in_=ap["wqT"][kc * 128:(kc + 1) * 128, :]),
                              16)
                SY(_)
                sld_sc[kc] = s_sld.n

                def _(v, kc=kc):
                    v.wait_ge(s_sld.h, sld_sc[kc])
                    if kc >= 2:
                        v.wait_ge(s_pe.h, pe_dq[kc - 2])
                    s_ve.inc(v.scalar_tensor_tensor(zsb[kc % 2][:, :],
                                                    zsb[kc % 2][:, :], -1.0,
                                                    scb[kc % 2][:, :],
                                                    OP.mult, OP.mult))
                V(_)
                ve_zs[kc] = s_ve.n

                def _(t, kc=kc):
                    t.wait_ge(s_ve.h, ve_zs[kc])
                    if kc >= 1:
                        t.wait_ge(s_ve.h, ve_dq[kc - 1])
                    t.matmul(psA[:, :], sel2[:, :], scb[kc % 2][:, :],
                             start=True, stop=True)
                    s_pe.inc(t.matmul(psB[:, :], sel2[:, :], zsb[kc % 2][:, :],
                                      start=True, stop=True))
                T(_)
                pe_dq[kc] = s_pe.n

                def _(v, kc=kc):
                    v.wait_ge(s_pe.h, pe_dq[kc])
                    s_ve.inc(v.tensor_copy(c0[:, :], i32b[kc % 2][:, :]))
                    v.wait_ge(s_ve.h, s_ve.n)
                    s_ve.inc(v.tensor_tensor(t1[:, :], c0[:, :], psA[:, :],
                                             OP.mult))
                    v.wait_ge(s_ve.h, s_ve.n)
                    s_ve.inc(v.tensor_tensor(wqd[:, kc, :], t1[:, :], psB[:, :],
                                             OP.add))
                V(_)
                ve_dq[kc] = s_ve.n

        with contextlib.ExitStack() as esf:
            psm = [esf.enter_context(nc.psum_tensor(f"psm{i}", [128, 512], F32))
                   for i in range(2)]
            ve_dqall = s_ve.n
            pe_f = {}
            ve_f = {}
            for hb in range(16):
                r0 = hb * 512

                def _(sy, hb=hb, r0=r0):
                    if hb >= 2:
                        sy.wait_ge(s_pe.h, pe_f[(hb - 2, 3)])
                    s_sld.inc(sy.dma_start(
                        out=xin[hb % 2][:, :, :],
                        in_=APc(ap["xn1t"].tensor, r0,
                                [[RT, 128], [RT * 128, C], [1, 512]])), 16)
                SY(_)
                sld_h = s_sld.n

                for mcc in range(4):
                    def _(t, hb=hb, mcc=mcc, sld_h=sld_h):
                        t.wait_ge(s_sld.h, sld_h)
                        if hb == 0 and mcc == 0:
                            t.wait_ge(s_ve.h, ve_dqall)
                        key = (hb, mcc - 2) if mcc >= 2 else (hb - 1, mcc + 2)
                        if key in ve_f:
                            t.wait_ge(s_ve.h, ve_f[key])
                        inst = None
                        for kc in range(C):
                            inst = t.matmul(psm[mcc % 2][:, :],
                                            wqd[:, kc, mcc * 128:(mcc + 1) * 128],
                                            xin[hb % 2][:, kc, :],
                                            start=(kc == 0), stop=(kc == C - 1))
                        s_pe.inc(inst)
                    T(_)
                    pe_f[(hb, mcc)] = s_pe.n

                    def _(v, hb=hb, mcc=mcc):
                        v.wait_ge(s_pe.h, pe_f[(hb, mcc)])
                        if (hb, mcc) not in [(0, 0), (0, 1)]:
                            pk = (hb, mcc - 2) if mcc >= 2 else (hb - 1, mcc + 2)
                            v.wait_ge(s_fst.h, fst_ring[pk])
                        s_ve.inc(v.tensor_copy(ebf[mcc % 2][:, :],
                                               psm[mcc % 2][:, :]))
                    V(_)
                    ve_f[(hb, mcc)] = s_ve.n

                    def _(g, hb=hb, mcc=mcc, r0=r0):
                        g.wait_ge(s_ve.h, ve_f[(hb, mcc)])
                        s_fst.inc(g.dma_start(
                            out=ap["dr_ffn"][mcc * 128:(mcc + 1) * 128,
                                             r0:r0 + 512],
                            in_=ebf[mcc % 2][:, :]), 16)
                    G(_)
                    if hb == 0 and mcc == 0:
                        fst_ring = {}
                    fst_ring[(hb, mcc)] = s_fst.n

        pe_fall = s_pe.n
        ve_fall = s_ve.n

        with contextlib.ExitStack() as esw:
            psm = [esw.enter_context(nc.psum_tensor(f"psw{i}", [128, 512], F32))
                   for i in range(2)]

            def _(g):
                s_gld.inc(g.dma_start(
                    out=wo_sb[:, :, :],
                    in_=ap["wo"].rearrange("(a p) n -> p a n", p=128)), 16)
            G(_)
            gld_wo = s_gld.n

            fst_all = s_fst.n
            pe_w = {}
            ve_w = {}
            st_w = {}
            for hb in range(16):
                r0 = hb * 512

                def _(sy, hb=hb, r0=r0):
                    if hb < 2:
                        sy.wait_ge(s_pe.h, pe_fall)
                    if hb >= 2:
                        sy.wait_ge(s_pe.h, pe_w[(hb - 2, 3)])
                    s_sld.inc(sy.dma_start(
                        out=xin[hb % 2][:, :, :],
                        in_=APc(ap["attnf"].tensor, r0,
                                [[RT, 128], [RT * 128, C], [1, 512]])), 16)
                SY(_)
                sld_h = s_sld.n

                for mcc in range(4):
                    def _(g, hb=hb, mcc=mcc, r0=r0):
                        if hb == 0 and mcc < 2:
                            g.wait_ge(s_fst.h, fst_all)
                        key = (hb, mcc - 2) if mcc >= 2 else (hb - 1, mcc + 2)
                        if key in ve_w:
                            g.wait_ge(s_ve.h, ve_w[key])
                        s_gld.inc(g.dma_start(
                            out=fch[mcc % 2][:, :],
                            in_=ap["dr_ffn"][mcc * 128:(mcc + 1) * 128,
                                             r0:r0 + 512]), 16)
                        s_gld.inc(g.dma_start(
                            out=xch[mcc % 2][:, :],
                            in_=ap["x1sl"][mcc * 128:(mcc + 1) * 128,
                                           r0:r0 + 512]), 16)
                    G(_)
                    gld_c = s_gld.n

                    def _(t, hb=hb, mcc=mcc, sld_h=sld_h):
                        t.wait_ge(s_sld.h, sld_h)
                        if hb == 0 and mcc == 0:
                            t.wait_ge(s_gld.h, gld_wo)
                            t.wait_ge(s_ve.h, ve_fall)
                        key = (hb, mcc - 2) if mcc >= 2 else (hb - 1, mcc + 2)
                        if key in ve_w:
                            t.wait_ge(s_ve.h, ve_w[key])
                        inst = None
                        for kc in range(C):
                            inst = t.matmul(psm[mcc % 2][:, :],
                                            wo_sb[:, kc, mcc * 128:(mcc + 1) * 128],
                                            xin[hb % 2][:, kc, :],
                                            start=(kc == 0), stop=(kc == C - 1))
                        s_pe.inc(inst)
                    T(_)
                    pe_w[(hb, mcc)] = s_pe.n

                    def _(v, hb=hb, mcc=mcc, gld_c=gld_c):
                        v.wait_ge(s_pe.h, pe_w[(hb, mcc)])
                        v.wait_ge(s_gld.h, gld_c)
                        if (hb, mcc) in st_w:
                            pass
                        pk = (hb, mcc - 2) if mcc >= 2 else (hb - 1, mcc + 2)
                        if pk in st_w:
                            v.wait_ge(s_st.h, st_w[pk])
                        s_ve.inc(v.tensor_scalar(evf[mcc % 2][:, :],
                                                 psm[mcc % 2][:, :],
                                                 bias1_sb[:, mcc:mcc + 1], None,
                                                 OP.add))
                        v.wait_ge(s_ve.h, s_ve.n)
                        s_ve.inc(v.tensor_tensor(evf[mcc % 2][:, :],
                                                 evf[mcc % 2][:, :],
                                                 fch[mcc % 2][:, :], OP.add))
                        v.wait_ge(s_ve.h, s_ve.n)
                        s_ve.inc(v.tensor_tensor(evf[mcc % 2][:, :],
                                                 evf[mcc % 2][:, :],
                                                 xch[mcc % 2][:, :], OP.add))
                    V(_)
                    ve_w[(hb, mcc)] = s_ve.n

                    def _(sy, hb=hb, mcc=mcc, r0=r0):
                        sy.wait_ge(s_ve.h, ve_w[(hb, mcc)])
                        s_st.inc(sy.dma_start(
                            out=ap["outsl"][mcc * 128:(mcc + 1) * 128, r0:r0 + 512],
                            in_=evf[mcc % 2][:, :]), 16)
                    SY(_)
                    st_w[(hb, mcc)] = s_st.n

            st_fin = s_st.n

            def _(sy):
                sy.wait_ge(s_st.h, st_fin)
            SY(_)


def stage_l3_inputs(inp, xn1t_full, attn_full, x1t_full, core):
    wq = np.asarray(inp["L1_ffn_weight_q"])
    sc = np.asarray(inp["L1_ffn_scale"])
    zp = np.asarray(inp["L1_ffn_zp"])
    nsl = slice(core * 512, (core + 1) * 512)
    scT = np.ascontiguousarray(sc[nsl, :].T)
    zpT = np.ascontiguousarray(zp[nsl, :].T)
    sel2 = np.zeros((2, 128), np.float32)
    sel2[0, 0:64] = 1.0
    sel2[1, 64:128] = 1.0
    return {
        "xn1t": xn1t_full,
        "attnf": attn_full,
        "x1sl": np.ascontiguousarray(x1t_full[nsl, :]),
        "wo": np.ascontiguousarray(np.asarray(inp["L1_attn_w_o"])[:, nsl]),
        "wqT": np.ascontiguousarray(wq[nsl, :].T),
        "sc2": np.ascontiguousarray(scT.reshape(C, 2, 512).transpose(1, 0, 2)),
        "zp2": np.ascontiguousarray(zpT.reshape(C, 2, 512).transpose(1, 0, 2)),
        "sel2": sel2,
        "bias1": np.ascontiguousarray(np.asarray(inp["L1_ffn_bias"])[nsl]),
    }


def stage_l0_inputs(inp, core):
    cst = _l0_consts()
    x = np.asarray(inp["L0_attn_u"]).reshape(8192, D)
    xt = np.ascontiguousarray(x[core * R:(core + 1) * R, :].T)

    vh_full = np.concatenate([np.asarray(inp["L0_ffn_Vh_top"]),
                              np.asarray(inp["L0_ffn_Vh_tail"])], axis=0)
    u_full = np.concatenate([np.asarray(inp["L0_ffn_U_top"]),
                             np.asarray(inp["L0_ffn_U_tail"])], axis=1)
    lam = np.asarray(inp["L0_attn_Lambda_ri"])
    p = np.asarray(inp["L0_attn_p_ri"])
    q = np.asarray(inp["L0_attn_q_ri"])
    vc = np.asarray(inp["L0_attn_Vc_ri"])
    ct = np.asarray(inp["L0_attn_Ct_ri"])
    d = {
        "xt": xt,
        "lam_re": np.ascontiguousarray(lam[:, 0]),
        "lam_im": np.ascontiguousarray(lam[:, 1]),
        "p_re": np.ascontiguousarray(p[:, 0]),
        "p_im": np.ascontiguousarray(p[:, 1]),
        "q_re": np.ascontiguousarray(q[:, 0]),
        "q_im": np.ascontiguousarray(q[:, 1]),
        "ct_re": np.ascontiguousarray(ct[0, :, 0]),
        "ct_im": np.ascontiguousarray(ct[0, :, 1]),
        "vct_re": np.ascontiguousarray(vc[:, :, 0].T),
        "vct_im": np.ascontiguousarray(vc[:, :, 1].T),
        "bp": np.ascontiguousarray(np.asarray(inp["L0_attn_Bp"])[:, 0]),
        "d_attn": np.asarray(inp["L0_attn_D"]),
        "log_step": np.asarray(inp["L0_attn_log_step"]),
        "g1c": cst["g1c"], "ckre": cst["ckre"], "ckim": cst["ckim"],
        "w64cat": cst["w64cat"], "twre": cst["twre"], "twim": cst["twim"],
        "w2cat": cst["w2cat"],
        "vht": np.ascontiguousarray(vh_full.T),
        "ut": np.ascontiguousarray(u_full.T),
        "s_top": np.asarray(inp["L0_ffn_S_top"]),
        "alpha": np.asarray(inp["L0_ffn_alpha"]),
        "beta": np.asarray(inp["L0_ffn_beta"]),
        "s_tail": np.asarray(inp["L0_ffn_S_tail"]),
        "vhr4t": np.ascontiguousarray(np.asarray(inp["L0_ffn_Vh_tail_r"]).T),
        "r4t": np.ascontiguousarray(np.asarray(inp["L0_ffn_R"]).T),
        "utr4t": np.ascontiguousarray(np.asarray(inp["L0_ffn_U_tail_r"]).T),
        "d4": np.asarray(inp["L0_ffn_D"]),
        "str4": np.asarray(inp["L0_ffn_S_tail_r"]),
        "bias0": np.asarray(inp["L0_ffn_bias"]),
    }
    return {k: np.ascontiguousarray(np.asarray(v, dtype=np.float32)) if
            np.asarray(v).dtype == np.float64 else np.ascontiguousarray(v)
            for k, v in d.items()}


def kernel(**inputs):
    cores = list(range(NCORE))

    nc0 = build_l0()
    im0 = [stage_l0_inputs(inputs, c) for c in cores]
    r0 = run_bass_kernel_spmd(nc0, im0, core_ids=cores)
    x1t_full = np.concatenate([r0.results[c]["x1t"] for c in cores], axis=1)
    xn1t_full = np.concatenate([r0.results[c]["xn1t"] for c in cores], axis=1)
    x1t_full = np.ascontiguousarray(x1t_full)
    xn1t_full = np.ascontiguousarray(xn1t_full)

    nc2 = build_l2()
    im2 = [stage_l2_inputs(inputs, xn1t_full, c) for c in cores]
    r2 = run_bass_kernel_spmd(nc2, im2, core_ids=cores)
    attn_full = np.ascontiguousarray(
        np.concatenate([r2.results[c]["attn"] for c in cores], axis=0))

    nc3 = build_l3()
    im3 = [stage_l3_inputs(inputs, xn1t_full, attn_full, x1t_full, c)
           for c in cores]
    r3 = run_bass_kernel_spmd(nc3, im3, core_ids=cores)
    outT = np.concatenate([r3.results[c]["outsl"] for c in cores], axis=0)
    return np.ascontiguousarray(outT.T).reshape(2, 4096, 4096)
